# version 06
# baseline (speedup 1.0000x reference)
import numpy as np
import concourse.bass as bass
import concourse.mybir as mybir
from concourse.bass_utils import run_bass_kernel_spmd
from contextlib import ExitStack

F32 = mybir.dt.float32
BF16 = mybir.dt.bfloat16
AF = mybir.ActivationFunctionType
ALU = mybir.AluOpType
AX = mybir.AxisListType

T = 2048
D = 1024
E = 8
FF = 4096
NH = 16
HD = 64
NF = 256
KC = D // 128
NT = T // 512
TC = T // 128
DEPTH = 1

ENGINES = ("sync", "tensor", "vector", "scalar", "gpsimd")


class Plan:

    def __init__(self, nc, strict=False):
        self.nc = nc
        self.strict = strict
        self.sems = {}
        self.owner = {}
        self.cnt = {}
        self.hw = {e: {} for e in ENGINES}
        self.ops = []
        self.fence_pending = {}

    def sem(self, name, owner):
        self.sems[name] = self.nc.alloc_semaphore(name)
        self.owner[name] = owner
        self.cnt[name] = 0

    def emit(self, engine, fn, waits=(), inc=None, amt=1):
        w = {}
        for t in waits:
            if t is None:
                continue
            s, v = t
            if v <= 0:
                continue
            if not self.strict and self.owner.get(s) == engine:
                continue
            w[s] = max(w.get(s, 0), v)
        for t in self.fence_pending.pop(engine, ()):
            s, v = t
            if v <= 0 or (not self.strict and self.owner.get(s) == engine):
                continue
            w[s] = max(w.get(s, 0), v)
        if self.strict:
            own = {"tensor": "T", "vector": "V", "scalar": "S",
                   "gpsimd": "G"}.get(engine)
            if own is not None:
                w[own] = max(w.get(own, 0), self.cnt[own])
                if inc is None:
                    inc = own
        w2 = []
        for s, v in sorted(w.items()):
            if self.hw[engine].get(s, 0) >= v:
                continue
            self.hw[engine][s] = v
            w2.append((s, v))
        tok = None
        if inc is not None:
            self.cnt[inc] += amt
            tok = (inc, self.cnt[inc])
        self.ops.append((engine, fn, tuple(w2), (inc, amt) if inc else None))
        return tok

    def barrier_self(self, engine, tok):
        s, v = tok
        h = self.sems[s]
        self.ops.append((engine, lambda e, h=h, v=v: e.wait_ge(h, v), (), None))

    def fence(self):
        snap = [(s, c) for s, c in self.cnt.items() if c > 0]
        for e in ENGINES:
            self.fence_pending[e] = list(snap)

    def flush(self, block):
        ops = self.ops
        self.ops = []
        for ename in ENGINES:
            eops = [o for o in ops if o[0] == ename]
            if not eops:
                continue

            def body(eng, eops=eops):
                for (_, fn, waits, inc) in eops:
                    for (s, v) in waits:
                        eng.wait_ge(self.sems[s], v)
                    ins = fn(eng)
                    if inc is not None:
                        assert ins is not None
                        ins.then_inc(self.sems[inc[0]], inc[1])

            getattr(block, ename)(body)

    def check(self):
        assert not self.ops, "unflushed ops"
        for e, hwm in self.hw.items():
            for s, v in hwm.items():
                assert v <= self.cnt[s], (e, s, v, self.cnt[s])


def _r64(h):
    half = h % 2
    return slice(64 * half, 64 * half + 64)


def build_program(nc, dump=(), strict=False):
    dumps = {}

    def din(name, shape):
        return nc.dram_tensor(name, shape, F32, kind="ExternalInput").ap()

    x_in = din("x", [T, D])
    out_hbm = nc.dram_tensor("out", [T, D], F32, kind="ExternalOutput").ap()
    L = []
    for d in range(DEPTH):
        L.append(dict(
            Wq=din(f"L{d}_Wq", [D, D]), Wk=din(f"L{d}_Wk", [D, D]),
            Wv=din(f"L{d}_Wv", [D, D]), Wo=din(f"L{d}_Wo", [D, D]),
            bq=din(f"L{d}_bq", [D]), bk=din(f"L{d}_bk", [D]),
            bv=din(f"L{d}_bv", [D]), bo=din(f"L{d}_bo", [D]),
            proj=din(f"L{d}_proj", [NH, NF, HD]),
            gate=din(f"L{d}_gate", [D, E]),
            w1=din(f"L{d}_w1", [E, D, FF]),
            w2=din(f"L{d}_w2", [E, FF, D]),
            w3=din(f"L{d}_w3", [E, D, FF]),
        ))

    plan = Plan(nc, strict=strict)
    plan.sem("WL", None)
    plan.sem("IO", None)
    plan.sem("T", "tensor")
    plan.sem("V", "vector")
    plan.sem("S", "scalar")
    plan.sem("G", "gpsimd")

    with ExitStack() as ctx:
        def sb(name, shape, dt):
            return ctx.enter_context(nc.sbuf_tensor(name, shape, dt))

        x_res = sb("x_res", [128, KC, T], F32)
        x_ln = sb("x_ln", [128, KC, T], BF16)
        scr = sb("scr", [128, T], F32)
        rowbuf = sb("rowbuf", [128, T], BF16)
        ones128 = sb("ones128", [128, 128], BF16)
        idb = sb("idb", [128, 128], BF16)
        idf = sb("idf", [128, 128], F32)
        iot8 = sb("iot8", [8, 128], F32)
        selE = sb("selE", [8, 128], BF16)
        iorow = sb("iorow", [128, 128], F32)
        pcol = sb("pcol", [128, 1], F32)
        bqf = sb("bqf", [128, KC], F32)
        bkf = sb("bkf", [128, KC], F32)
        bvf = sb("bvf", [128, KC], F32)
        bof = sb("bof", [128, KC], F32)
        gate_bf = sb("gate_bf", [128, KC, E], BF16)
        logits = sb("logits", [128, TC, E], F32)
        wtok = sb("wtok", [128, TC, E], BF16)
        rt = sb("rt", [128, 6, TC, E], F32)
        rc = sb("rc", [128, 6, TC, 1], F32)
        wTr = sb("wTr", [8, TC, 128], BF16)
        xio = scr[:, 0:1024]
        arena = ctx.enter_context(nc.psum_tensor("arena", [128, 4096], F32))

        def A_(p0, p1, c0, c1):
            return arena[p0:p1, c0:c1]

        def Abf(p0, p1, c0, c1):
            return arena[p0:p1, c0:c1].bitcast(BF16)

        def MM(o, l, r, st, sp):
            return lambda e: e.matmul(o, l, r, start=st, stop=sp,
                                      skip_group_check=True)

        def TR(o, i, ident):
            return lambda e: e.transpose(o, i, ident)

        def TT(o, a, b, op):
            return lambda e: e.tensor_tensor(o, a, b, op)

        def STT(o, a, s, b, op0, op1):
            return lambda e: e.scalar_tensor_tensor(o, a, s, b, op0, op1)

        def TS(o, a, s1, s2, op0, op1=ALU.bypass):
            return lambda e: e.tensor_scalar(o, a, s1, s2, op0, op1)

        def CP(o, i):
            return lambda e: e.tensor_copy(o, i)

        def ACT(o, i, f, bias=0.0, scale=1.0):
            return lambda e: e.activation(o, i, f, bias=bias, scale=scale)

        def RED(o, i, op):
            return lambda e: e.tensor_reduce(o, i, AX.X, op)

        def RECIP(o, i):
            return lambda e: e.reciprocal(o, i)

        def DMA(o, i, slow=False):
            if slow:
                return lambda e: e.dma_start(out=o, in_=i,
                                             allow_slow_non_contiguous=True)
            return lambda e: e.dma_start(out=o, in_=i)

        def dump_tensor(tag, src_ap, shape, dt=F32):
            if tag not in dump:
                return
            dptr = nc.dram_tensor("dump_" + tag, shape, dt,
                                  kind="ExternalOutput").ap()
            dumps[tag] = dptr
            plan.fence()
            idx = tuple(slice(None) for _ in shape)
            plan.emit("sync", DMA(dptr[idx], src_ap), inc="IO", amt=16)
            plan.fence()

        tg_iot = plan.emit("gpsimd", lambda e: e.iota(
            iot8[:], [[0, 128]], channel_multiplier=1,
            allow_small_or_imprecise_dtypes=True), inc="G")
        tg_row = plan.emit("gpsimd", lambda e: e.iota(
            iorow[:], [[1, 128]], channel_multiplier=0,
            allow_small_or_imprecise_dtypes=True), inc="G")
        tg_col = plan.emit("gpsimd", lambda e: e.iota(
            pcol[:], [[0, 1]], channel_multiplier=1,
            allow_small_or_imprecise_dtypes=True), inc="G")
        tg = plan.emit("gpsimd", lambda e: e.memset(ones128[:], 1.0), inc="G")
        plan.emit("vector", TS(idf[:], iorow[:], pcol[:], None, ALU.is_equal),
                  waits=[tg_row, tg_col], inc="V")
        tid = plan.emit("vector", CP(idb[:], idf[:]), inc="V")

        lw0 = L[0]
        for (dst, src) in ((bqf, lw0["bq"]), (bkf, lw0["bk"]),
                           (bvf, lw0["bv"]), (bof, lw0["bo"])):
            plan.emit("sync", DMA(dst[:], src.rearrange("(c p) -> p c", p=128),
                                  slow=True), inc="IO", amt=16)
        t_gate = plan.emit("gpsimd", DMA(
            gate_bf[:], lw0["gate"].rearrange("(c k) e -> k c e", k=128)),
            inc="WL", amt=16)

        ev_tok = [None, None, None, None]
        tr_tok = [None, None, None, None]
        for tc in range(TC):
            t_in = plan.emit("sync", DMA(xio, x_in[tc * 128:(tc + 1) * 128, :]),
                             waits=list(tr_tok), inc="IO", amt=16)
            for c in range(KC):
                b = c % 4
                tr_tok[b] = plan.emit("tensor", TR(
                    A_(0, 128, b * 512, b * 512 + 128),
                    xio[:, c * 128:(c + 1) * 128], idf[:]),
                    waits=[t_in, tid, ev_tok[b]], inc="T")
                ev_tok[b] = plan.emit("vector", CP(
                    x_res[:, c, tc * 128:(tc + 1) * 128],
                    A_(0, 128, b * 512, b * 512 + 128)),
                    waits=[tr_tok[b]], inc="V")
        plan.fence()

        def emit_ln():
            sq = (scr[:, 0:1024].bitcast(BF16), scr[:, 1024:2048].bitcast(BF16))
            t_sq_cons = [None, None]
            t_xbf = [None] * KC
            t_sq = [None] * KC
            t_st = None
            for c in range(KC):
                t_xbf[c] = plan.emit("scalar", ACT(x_ln[:, c, :], x_res[:, c, :],
                                                   AF.Copy), inc="S")
                t_sq[c] = plan.emit("scalar", ACT(sq[c % 2], x_res[:, c, :],
                                                  AF.Square),
                                    waits=[t_sq_cons[c % 2]], inc="S")
                for nt in range(NT):
                    plan.emit("tensor", MM(
                        A_(0, 1, nt * 512, (nt + 1) * 512), ones128[:, 0:1],
                        x_ln[:, c, nt * 512:(nt + 1) * 512],
                        c == 0, c == KC - 1), waits=[t_xbf[c]])
                tok = None
                for nt in range(NT):
                    tok = plan.emit("tensor", MM(
                        A_(0, 1, 2048 + nt * 512, 2048 + (nt + 1) * 512),
                        ones128[:, 0:1],
                        sq[c % 2][:, nt * 512:(nt + 1) * 512],
                        c == 0, c == KC - 1),
                        waits=[t_sq[c]], inc="T" if nt == NT - 1 else None)
                t_sq_cons[c % 2] = tok
                t_st = tok
            var = scr[0:1, 0:2048]
            row = rowbuf[0:1, :]
            tva = plan.emit("vector", TS(var, A_(0, 1, 2048, 4096), 1.0 / D,
                                         None, ALU.mult), waits=[t_st], inc="V")
            tmb = plan.emit("vector", TS(row, A_(0, 1, 0, 2048), 1.0 / D,
                                         None, ALU.mult), waits=[t_st], inc="V")
            tb = []
            tbm = None
            for nt in range(NT):
                tbm = plan.emit("tensor", MM(
                    A_(0, 128, nt * 512, (nt + 1) * 512), ones128[0:1, :],
                    row[:, nt * 512:(nt + 1) * 512], True, True),
                    waits=[tmb, tva], inc="T")
                tb.append(tbm)
            tms = plan.emit("scalar", ACT(row, row, AF.Square), waits=[tbm],
                            inc="S")
            tv2 = plan.emit("vector", TT(var, var, row, ALU.subtract),
                            waits=[tms], inc="V")
            tv3 = plan.emit("vector", TS(var, var, 1e-5, None, ALU.add),
                            inc="V")
            plan.emit("scalar", ACT(var, var, AF.Ln), waits=[tv3], inc="S")
            trs = plan.emit("scalar", ACT(row, var, AF.Exp, scale=-0.5),
                            inc="S")
            for nt in range(NT):
                tb.append(plan.emit("tensor", MM(
                    A_(0, 128, 2048 + nt * 512, 2048 + (nt + 1) * 512),
                    ones128[0:1, :], row[:, nt * 512:(nt + 1) * 512],
                    True, True), waits=[trs], inc="T"))
            t_last = None
            for c in range(KC):
                plan.emit("vector", TT(scr[:, :], x_res[:, c, :],
                                       A_(0, 128, 0, 2048), ALU.subtract),
                          waits=tb, inc="V")
                t_last = plan.emit("vector", TT(x_ln[:, c, :], scr[:, :],
                                                A_(0, 128, 2048, 4096),
                                                ALU.mult), inc="V")
            return t_last

        def emit_routing(t_ln):
            tev = [None, None]
            for tc in range(TC):
                slot = A_(0, 128, (tc % 2) * 512, (tc % 2) * 512 + E)
                tok = None
                for kc in range(KC):
                    tok = plan.emit("tensor", MM(
                        slot, x_ln[:, kc, tc * 128:(tc + 1) * 128],
                        gate_bf[:, kc, :], kc == 0, kc == KC - 1),
                        waits=[t_ln, t_gate, tev[tc % 2]],
                        inc="T" if kc == KC - 1 else None)
                tev[tc % 2] = plan.emit("vector", CP(logits[:, tc, :], slot),
                                        waits=[tok], inc="V")

            lg = logits[:]
            b = lambda i: rc[:, i, :, :].broadcast_to([128, TC, E])
            r_ = lambda i: rt[:, i, :, :]
            c_ = lambda i: rc[:, i, :, :]
            def V(fn, waits=(), bar=False):
                tok = plan.emit("vector", fn, waits=waits, inc="V")
                if bar:
                    plan.barrier_self("vector", tok)
                return tok
            S = lambda fn, waits=(): plan.emit("scalar", fn, waits=waits,
                                               inc="S")
            V(RED(c_(0), lg, ALU.max), bar=True)
            V(TT(r_(0), lg, b(0), ALU.is_equal))
            V(TS(r_(1), r_(0), 1e30, None, ALU.mult))
            V(TT(r_(1), lg, r_(1), ALU.subtract))
            V(RED(c_(1), r_(1), ALU.max), bar=True)
            tsa = S(ACT(r_(2), lg, AF.Abs), waits=[tev[0], tev[1]])
            V(TT(r_(3), r_(2), b(0), ALU.max), waits=[tsa])
            V(TT(r_(4), b(0), lg, ALU.subtract))
            tv = V(STT(r_(3), r_(3), 0.02, r_(4), ALU.mult, ALU.subtract))
            V(TS(r_(3), r_(3), 0.0, None, ALU.is_ge))
            ts1 = S(ACT(r_(4), r_(4), AF.Exp, scale=-1.0), waits=[tv])
            V(TT(r_(4), r_(3), r_(4), ALU.mult), waits=[ts1])
            V(RED(c_(4), r_(4), ALU.add), bar=True)
            V(RECIP(c_(2), c_(4)), bar=True)
            V(TT(r_(3), r_(2), b(1), ALU.max))
            tv2 = V(TT(r_(5), b(1), lg, ALU.subtract))
            V(STT(r_(3), r_(3), 0.02, r_(5), ALU.mult, ALU.subtract))
            V(TS(r_(3), r_(3), 0.0, None, ALU.is_ge))
            V(TT(r_(2), r_(3), r_(0), ALU.mult))
            tv3 = V(TT(r_(3), r_(3), r_(2), ALU.subtract))
            ts2 = S(ACT(r_(5), r_(5), AF.Exp, scale=-1.0), waits=[tv3])
            V(TT(r_(5), r_(3), r_(5), ALU.mult), waits=[ts2])
            V(RED(c_(5), r_(5), ALU.add), bar=True)
            V(RECIP(c_(3), c_(5)), bar=True)
            V(TT(r_(1), r_(1), b(1), ALU.is_equal))
            V(TT(r_(0), r_(0), b(2), ALU.mult))
            V(TT(r_(1), r_(1), b(3), ALU.mult))
            t_w = V(TT(wtok[:], r_(0), r_(1), ALU.add))
            tevw = [None, None]
            t_wT = None
            for tc in range(TC):
                col = 3072 + (tc % 2) * 512
                slot = Abf(0, 8, col, col + 64)
                tok = plan.emit("tensor", TR(slot, wtok[:, tc, :], idb[:]),
                                waits=[t_w, tid, tevw[tc % 2]], inc="T")
                tevw[tc % 2] = t_wT = plan.emit(
                    "vector", CP(wTr[:, tc, :], slot), waits=[tok], inc="V")
            return t_wT

        def emit_ffn(lw, t_wT):
            h_sb = ffn["h_sb"]
            w1s, w3s, w2s = ffn["w1s"], ffn["w3s"], ffn["w2s"]
            wb = ffn["wb"]
            wT_flat = wTr[:].rearrange("p a b -> p (a b)")
            t_hev = [None] * 16
            t_silu = [None] * 16
            st = dict(h1_last=None, h3_last=None, y_last=None,
                      yev_b03=None, yev_b47=None, ymm_last=None,
                      yev_last=None, sel_used=None)
            for e_ in range(E):
                t_sel = plan.emit("vector", TS(selE[:], iot8[:], float(e_),
                                               None, ALU.is_equal),
                                  waits=[st["sel_used"], tg_iot], inc="V")
                twb_mm = None
                for nt in range(NT):
                    twb_mm = plan.emit("tensor", MM(
                        A_(0, 128, 2048 + nt * 512, 2048 + (nt + 1) * 512),
                        selE[:],
                        wT_flat[:, nt * 512:(nt + 1) * 512], True, True),
                        waits=[t_wT, t_sel, t_hev[15], st["yev_b47"]],
                        inc="T" if nt == NT - 1 else None)
                st["sel_used"] = twb_mm
                t_wb = plan.emit("vector", TS(wb[:], A_(0, 128, 2048, 4096),
                                              0.5, None, ALU.mult),
                                 waits=[twb_mm], inc="V")
                for fc in range(2):
                    t_w1 = [None] * 16
                    t_w3 = [None] * 16
                    t_h1 = [None] * 16
                    t_h3 = [None] * 16
                    for fs in range(16):
                        off = fc * 2048 + fs * 128
                        t_w1[fs] = plan.emit("gpsimd", DMA(
                            w1s[:, fs % 2, :, :],
                            lw["w1"][e_, :, off:off + 128].rearrange(
                                "(kc k) f -> k kc f", k=128)),
                            waits=[t_h1[fs - 2] if fs >= 2 else st["h1_last"]],
                            inc="WL", amt=16)
                        t_w3[fs] = plan.emit("gpsimd", DMA(
                            w3s[:, fs % 2, :, :],
                            lw["w3"][e_, :, off:off + 128].rearrange(
                                "(kc k) f -> k kc f", k=128)),
                            waits=[t_h3[fs - 2] if fs >= 2 else st["h3_last"]],
                            inc="WL", amt=16)
                        tok = None
                        for kc in range(KC):
                            for nt in range(NT):
                                tok = plan.emit("tensor", MM(
                                    A_(0, 128, nt * 512, (nt + 1) * 512),
                                    w1s[:, fs % 2, kc, :],
                                    x_ln[:, kc, nt * 512:(nt + 1) * 512],
                                    kc == 0, kc == KC - 1),
                                    waits=[t_w1[fs],
                                           t_silu[fs - 1] if fs else t_silu[15],
                                           st["yev_b03"] if fs == 0 else None,
                                           t_wT if (e_ == 0 and fc == 0
                                                    and fs == 0) else None],
                                    inc="T" if (kc == KC - 1 and nt == NT - 1)
                                    else None)
                        t_h1[fs] = tok
                        tok = None
                        for kc in range(KC):
                            for nt in range(NT):
                                tok = plan.emit("tensor", MM(
                                    A_(0, 128, 2048 + nt * 512,
                                       2048 + (nt + 1) * 512),
                                    w3s[:, fs % 2, kc, :],
                                    x_ln[:, kc, nt * 512:(nt + 1) * 512],
                                    kc == 0, kc == KC - 1),
                                    waits=[t_w3[fs],
                                           t_wb if fs == 0 else None,
                                           t_hev[fs - 1] if fs else t_hev[15],
                                           st["yev_b47"] if fs == 0 else None],
                                    inc="T" if (kc == KC - 1 and nt == NT - 1)
                                    else None)
                        t_h3[fs] = tok
                        t_silu[fs] = plan.emit("scalar", ACT(
                            h_sb[:, fs, :], A_(0, 128, 0, 2048), AF.Silu),
                            waits=[t_h1[fs],
                                   st["ymm_last"] if fs == 0 else None],
                            inc="S")
                        t_hev[fs] = plan.emit("vector", TT(
                            h_sb[:, fs, :], h_sb[:, fs, :],
                            A_(0, 128, 2048, 4096), ALU.mult),
                            waits=[t_silu[fs], t_h3[fs]],
                            inc="V")
                    st["h1_last"] = t_h1[14]
                    st["h3_last"] = t_h3[14]
                    ymm = [None] * KC
                    yev = [None] * KC
                    t_w2 = [None] * KC
                    for hoc in range(KC):
                        t_w2[hoc] = plan.emit("gpsimd", DMA(
                            w2s[:, hoc % 2, :, :],
                            lw["w2"][e_, fc * 2048:(fc + 1) * 2048,
                                     hoc * 128:(hoc + 1) * 128].rearrange(
                                "(fs k) h -> k fs h", k=128)),
                            waits=[ymm[hoc - 2] if hoc >= 2 else st["y_last"]],
                            inc="WL", amt=16)
                        base = (hoc % 2) * 2048
                        tok = None
                        for fs in range(16):
                            for nt in range(NT):
                                tok = plan.emit("tensor", MM(
                                    A_(0, 128, base + nt * 512,
                                       base + (nt + 1) * 512),
                                    w2s[:, hoc % 2, fs, :],
                                    h_sb[:, fs, nt * 512:(nt + 1) * 512],
                                    fs == 0, fs == 15),
                                    waits=[t_w2[hoc], t_hev[15],
                                           yev[hoc - 2] if hoc >= 2 else None,
                                           t_silu[15] if hoc == 0 else None],
                                    inc="T" if (fs == 15 and nt == NT - 1)
                                    else None)
                        ymm[hoc] = tok
                        plan.emit("vector", TT(
                            scr[:, :], A_(0, 128, base, base + 2048), wb[:],
                            ALU.mult), waits=[tok, t_wb], inc="V")
                        yev[hoc] = plan.emit("vector", TT(
                            x_res[:, hoc, :], x_res[:, hoc, :], scr[:, :],
                            ALU.add), inc="V")
                    st["y_last"] = ymm[KC - 1]
                    st["ymm_last"] = ymm[KC - 1]
                    st["yev_b03"] = yev[KC - 2]
                    st["yev_b47"] = yev[KC - 1]
                    st["yev_last"] = yev[KC - 1]
            return st["yev_last"]

        def emit_attn(lw, t_ln):
            A_T = att["A"]
            qT, kT, vT = att["qT"], att["kT"], att["vT"]
            qkv_sb = att["qkv_sb"]
            Qp, Kp, Vx, kvx = att["Qp"], att["Kp"], att["Vx"], att["kvx"]
            pT = att["pT"]
            wqh, wkh, wvh, ph = att["wqh"], att["wkh"], att["wvh"], att["ph"]
            woc = att["woc"]
            tg1 = plan.emit("gpsimd", lambda e: e.memset(Vx[:, :, 64:65], 1.0),
                            inc="G")
            H = [dict() for _ in range(NH)]

            def hd(i, key, default=None):
                if i < 0:
                    return default
                return H[i].get(key, default)

            def proj(h):
                hb = h % 2
                c = h // 2
                r = _r64(h)
                s = H[h]
                tq = plan.emit("gpsimd", DMA(wqh[:, hb, :, :], lw["Wq"][
                    :, h * 64:(h + 1) * 64].rearrange("(kc k) m -> k kc m",
                                                      k=128)),
                    waits=[hd(h - 2, "t_qmm")], inc="WL", amt=16)
                tk = plan.emit("gpsimd", DMA(wkh[:, hb, :, :], lw["Wk"][
                    :, h * 64:(h + 1) * 64].rearrange("(kc k) m -> k kc m",
                                                      k=128)),
                    waits=[hd(h - 2, "t_kmm")], inc="WL", amt=16)
                tv = plan.emit("gpsimd", DMA(wvh[:, hb, :, :], lw["Wv"][
                    :, h * 64:(h + 1) * 64].rearrange("(kc k) m -> k kc m",
                                                      k=128)),
                    waits=[hd(h - 2, "t_vmm")], inc="WL", amt=16)
                tp = plan.emit("gpsimd", DMA(ph[:, hb, :, :], lw["proj"][
                    h].rearrange("(pc k) d -> k pc d", k=128)),
                    waits=[hd(h - 2, "t_ptr")], inc="WL", amt=16)
                prev_b03 = hd(h - 1, "t_vev", t_ln if h <= 1 else None)
                t_ptev = None
                for pc in range(2):
                    slot = Abf(r.start, r.start + 64, pc * 512, pc * 512 + 64)
                    tok = plan.emit("tensor", TR(slot, ph[:, hb, pc, :],
                                                 idb[:]),
                                    waits=[tp, tid, prev_b03,
                                           hd(h - 2, "t_kpmm")], inc="T")
                    s["t_ptr"] = tok
                    t_ptev = plan.emit("vector", CP(pT[r, pc, :], slot),
                                       waits=[tok], inc="V")
                s["t_ptev"] = t_ptev
                yield
                tok = None
                for kc in range(KC):
                    for nt in range(NT):
                        tok = plan.emit("tensor", MM(
                            A_(r.start, r.stop, nt * 512, (nt + 1) * 512),
                            wqh[:, hb, kc, :],
                            x_ln[:, kc, nt * 512:(nt + 1) * 512],
                            kc == 0, kc == KC - 1),
                            waits=[tq, t_ptev],
                            inc="T" if (kc == KC - 1 and nt == NT - 1) else None)
                s["t_qmm"] = tok
                s["t_qev"] = plan.emit("vector", TS(
                    qT[r, :], A_(r.start, r.stop, 0, 2048),
                    bqf[r, c:c + 1], None, ALU.add),
                    waits=[tok, hd(h - 2, "t_qp1")], inc="V")
                yield
                tok = None
                for kc in range(KC):
                    for nt in range(NT):
                        tok = plan.emit("tensor", MM(
                            A_(r.start, r.stop, nt * 512, (nt + 1) * 512),
                            wkh[:, hb, kc, :],
                            x_ln[:, kc, nt * 512:(nt + 1) * 512],
                            kc == 0, kc == KC - 1),
                            waits=[tk, s["t_qev"]],
                            inc="T" if (kc == KC - 1 and nt == NT - 1) else None)
                s["t_kmm"] = tok
                s["t_kev"] = plan.emit("scalar", ACT(
                    kT[r, :], A_(r.start, r.stop, 0, 2048), AF.Identity,
                    bias=bkf[r, c:c + 1]),
                    waits=[tok, hd(h - 2, "t_kpmm")], inc="S")
                yield
                tok = None
                for kc in range(KC):
                    for nt in range(NT):
                        tok = plan.emit("tensor", MM(
                            A_(r.start, r.stop, nt * 512, (nt + 1) * 512),
                            wvh[:, hb, kc, :],
                            x_ln[:, kc, nt * 512:(nt + 1) * 512],
                            kc == 0, kc == KC - 1),
                            waits=[tv, s["t_kev"]],
                            inc="T" if (kc == KC - 1 and nt == NT - 1) else None)
                s["t_vmm"] = tok
                s["t_vev"] = plan.emit("scalar", ACT(
                    vT[r, :], A_(r.start, r.stop, 0, 2048), AF.Identity,
                    bias=bvf[r, c:c + 1]),
                    waits=[tok, hd(h - 2, "t_vtmm")], inc="S")
                yield

            def chain(h):
                c = h // 2
                r = _r64(h)
                s = H[h]
                C = 2048

                def norm2(src, t_src, t_row_free, t_banks_free, extra_sq_wait):
                    sqv = scr[r, 0:1024].bitcast(BF16)
                    tsq = plan.emit("scalar", ACT(sqv, src, AF.Square),
                                    waits=[t_src, t_row_free, extra_sq_wait],
                                    inc="S")
                    tok = None
                    for nt in range(NT):
                        tok = plan.emit("tensor", MM(
                            A_(0, 1, C + nt * 512, C + (nt + 1) * 512),
                            ones128[r, 0:1],
                            sqv[:, nt * 512:(nt + 1) * 512], True, True),
                            waits=[tsq, t_banks_free],
                            inc="T" if nt == NT - 1 else None)
                    t_st = tok
                    rrow = scr[0:1, 0:2048]
                    t1 = plan.emit("scalar", ACT(rrow, A_(0, 1, C, C + 2048),
                                                 AF.Ln), waits=[tok], inc="S")
                    plan.emit("scalar", ACT(rrow, rrow, AF.Exp, scale=0.5),
                              inc="S")
                    t3 = plan.emit("vector", TS(rrow, rrow, 1e-8, None,
                                                ALU.add),
                                   waits=[("S", plan.cnt["S"])], inc="V")
                    plan.emit("scalar", ACT(rrow, rrow, AF.Ln), waits=[t3],
                              inc="S")
                    inv = rowbuf[0:1, :]
                    t4 = plan.emit("scalar", ACT(inv, rrow, AF.Exp,
                                                 scale=-1.0), inc="S")
                    tokb = None
                    for nt in range(NT):
                        tokb = plan.emit("tensor", MM(
                            A_(r.start, r.stop, C + nt * 512,
                               C + (nt + 1) * 512),
                            ones128[0:1, 0:64],
                            inv[:, nt * 512:(nt + 1) * 512], True, True),
                            waits=[t4, t1], inc="T" if nt == NT - 1 else None)
                    tn = plan.emit("vector", TT(
                        src, src, A_(r.start, r.stop, C, C + 2048), ALU.mult),
                        waits=[tokb], inc="V")
                    return tn, t_st

                t_qn, t_stq = norm2(qT[r, :], s["t_qev"],
                                    hd(h - 1, "t_nrm_done"),
                                    hd(h - 1, "t_Aev"),
                                    hd(h - 2, "t_stk"))
                yield
                t_kn, t_stk = norm2(kT[r, :], s["t_kev"], t_qn, t_qn, t_qn)
                s["t_stk"] = t_stk
                yield
                t_qp = [None, None]
                for pc in range(2):
                    tok = None
                    for nt in range(NT):
                        tok = plan.emit("tensor", MM(
                            A_(0, 128, C + nt * 512, C + (nt + 1) * 512),
                            pT[r, pc, :],
                            qT[r, nt * 512:(nt + 1) * 512], True, True),
                            waits=[t_qn, s["t_ptev"],
                                   t_kn if pc == 0 else t_qp[0]],
                            inc="T" if nt == NT - 1 else None)
                    t_qp[pc] = plan.emit("vector", TS(
                        Qp[:, pc, :], A_(0, 128, C, C + 2048),
                        0.0, 0.1, ALU.max, ALU.add), waits=[tok], inc="V")
                    if pc == 0:
                        yield
                s["t_qp1"] = t_qp[1]
                yield
                t_kpev = [None] * TC
                tok = None
                pT_flat = pT[r, :, :].rearrange("p a b -> p (a b)")
                for tc in range(TC):
                    col = C + (tc % 2) * 512
                    tok = plan.emit("tensor", MM(
                        A_(0, 128, col, col + 256),
                        kT[r, tc * 128:(tc + 1) * 128], pT_flat, True, True),
                        waits=[t_kn, t_qp[1],
                               t_kpev[tc - 2] if tc >= 2 else None], inc="T")
                    t_kpev[tc] = plan.emit("vector", TS(
                        Kp[:, tc, :], A_(0, 128, col, col + 256),
                        0.0, 0.1, ALU.max, ALU.add), waits=[tok], inc="V")
                s["t_kpmm"] = tok
                yield
                rid = idb[r, r]
                t_vtev = [None] * TC
                tok = None
                for tc in range(TC):
                    col = C + 1024 + (tc % 2) * 512
                    slot = Abf(0, 128, col, col + 32)
                    tok = plan.emit("tensor", TR(
                        slot, vT[r, tc * 128:(tc + 1) * 128], rid),
                        waits=[s["t_vev"], t_qp[1],
                               t_vtev[tc - 2] if tc >= 2 else None], inc="T")
                    t_vtev[tc] = plan.emit("vector", CP(
                        Vx[:, tc, 0:64], slot), waits=[tok, tg1], inc="V")
                s["t_vtmm"] = tok
                yield
                t_kvev = [None, None]
                for fc in range(2):
                    col = C + fc * 512
                    tok = None
                    for tc in range(TC):
                        tok = plan.emit("tensor", MM(
                            A_(0, 128, col, col + 65),
                            Kp[:, tc, fc * 128:(fc + 1) * 128],
                            Vx[:, tc, :], tc == 0, tc == TC - 1),
                            waits=[t_kpev[TC - 1] if tc == 0 else t_kpev[tc],
                                   t_vtev[tc]],
                            inc="T" if tc == TC - 1 else None)
                    t_kvev[fc] = plan.emit("vector", CP(
                        kvx[:, fc, :], A_(0, 128, col, col + 65)),
                        waits=[tok], inc="V")
                yield
                tok = None
                for fc in range(2):
                    for nt in range(NT):
                        tok = plan.emit("tensor", MM(
                            A_(r.start, r.stop, C + nt * 512,
                               C + (nt + 1) * 512),
                            kvx[:, fc, 0:64],
                            Qp[:, fc, nt * 512:(nt + 1) * 512],
                            fc == 0, fc == 1),
                            waits=[t_kvev[fc], t_kpev[TC - 1],
                                   t_vtev[TC - 1]],
                            inc="T" if (fc == 1 and nt == NT - 1) else None)
                t_omm = tok
                t_qkv = plan.emit("scalar", ACT(
                    qkv_sb[r, :], A_(r.start, r.stop, C, C + 2048), AF.Copy),
                    waits=[t_omm], inc="S")
                yield
                tok = None
                for fc in range(2):
                    for nt in range(NT):
                        tok = plan.emit("tensor", MM(
                            A_(0, 1, C + nt * 512, C + (nt + 1) * 512),
                            kvx[:, fc, 64:65],
                            Qp[:, fc, nt * 512:(nt + 1) * 512],
                            fc == 0, fc == 1),
                            waits=[t_qkv],
                            inc="T" if (fc == 1 and nt == NT - 1) else None)
                nr = scr[0:1, 0:2048]
                tn0 = plan.emit("vector", TS(nr, A_(0, 1, C, C + 2048), 1e-6,
                                             None, ALU.max), waits=[tok],
                                inc="V")
                plan.emit("scalar", ACT(nr, nr, AF.Ln), waits=[tn0], inc="S")
                inv_n = rowbuf[0:1, :]
                tin = plan.emit("scalar", ACT(inv_n, nr, AF.Exp, scale=-1.0),
                                inc="S")
                s["t_nrm_done"] = tin
                tok = None
                for nt in range(NT):
                    tok = plan.emit("tensor", MM(
                        A_(r.start, r.stop, C + nt * 512, C + (nt + 1) * 512),
                        ones128[0:1, 0:64],
                        inv_n[:, nt * 512:(nt + 1) * 512], True, True),
                        waits=[tin], inc="T" if nt == NT - 1 else None)
                s["t_Aev"] = plan.emit("vector", TT(
                    A_T[r, c, :], qkv_sb[r, :],
                    A_(r.start, r.stop, C, C + 2048), ALU.mult),
                    waits=[t_qkv, tok], inc="V")
                yield

            def drain(g):
                if g is None:
                    return
                for _ in g:
                    pass

            def alternate(g1, g2):
                gens = [g for g in (g1, g2) if g is not None]
                while gens:
                    nxt = []
                    for g in gens:
                        try:
                            next(g)
                            nxt.append(g)
                        except StopIteration:
                            pass
                    gens = nxt

            drain(proj(0))
            drain(proj(1))
            for h in range(NH):
                g = chain(h)
                for _ in range(5):
                    next(g)
                alternate(g, proj(h + 2) if h + 2 < NH else None)

            t_Aev = H[NH - 1]["t_Aev"]
            t_woev = [None] * KC
            t_womm = [None] * KC
            for oc in range(KC):
                hb = oc % 2
                two = plan.emit("gpsimd", DMA(
                    woc[:, hb, :, :], lw["Wo"][:, oc * 128:(oc + 1) * 128]
                    .rearrange("(kc k) m -> k kc m", k=128)),
                    waits=[t_womm[oc - 2] if oc >= 2 else None],
                    inc="WL", amt=16)
                base = hb * 2048
                tok = None
                for kc in range(KC):
                    for nt in range(NT):
                        tok = plan.emit("tensor", MM(
                            A_(0, 128, base + nt * 512, base + (nt + 1) * 512),
                            woc[:, hb, kc, :],
                            A_T[:, kc, nt * 512:(nt + 1) * 512],
                            kc == 0, kc == KC - 1),
                            waits=[two, t_Aev,
                                   H[NH - 1]["t_vev"] if oc == 0 else None,
                                   t_woev[oc - 2] if oc >= 2 else None],
                            inc="T" if (kc == KC - 1 and nt == NT - 1) else None)
                t_womm[oc] = tok
                plan.emit("vector", TS(scr[:, :], A_(0, 128, base, base + 2048),
                                       bof[:, oc:oc + 1], None, ALU.add),
                          waits=[tok], inc="V")
                t_woev[oc] = plan.emit("vector", TT(
                    x_res[:, oc, :], x_res[:, oc, :], scr[:, :], ALU.add),
                    inc="V")
            return t_woev[-1]

        with nc.allow_low_precision(reason="bf16 operands are blessed for "
                                    "this problem (fp32 accumulation kept)"):
            with nc.Block("p0") as blk:
                plan.flush(blk)

            for d in range(DEPTH):
                lw = L[d]
                with ExitStack() as fctx:
                    ffn = dict(
                        h_sb=fctx.enter_context(
                            nc.sbuf_tensor("h_sb", [128, 16, T], BF16)),
                        w1s=fctx.enter_context(
                            nc.sbuf_tensor("w1s", [128, 2, KC, 128], BF16)),
                        w3s=fctx.enter_context(
                            nc.sbuf_tensor("w3s", [128, 2, KC, 128], BF16)),
                        w2s=fctx.enter_context(
                            nc.sbuf_tensor("w2s", [128, 2, 16, 128], BF16)),
                        wb=fctx.enter_context(
                            nc.sbuf_tensor("wb", [128, T], BF16)),
                    )
                    t_ln = emit_ln()
                    dump_tensor(f"ln1_{d}", x_ln[:], [128, KC, T], BF16)
                    t_wT = emit_routing(t_ln)
                    dump_tensor(f"lg1_{d}", logits[:], [128, TC, E])
                    dump_tensor(f"rt_{d}", rt[:], [128, 6, TC, E])
                    dump_tensor(f"rc_{d}", rc[:], [128, 6, TC, 1])
                    dump_tensor(f"w1r_{d}", wtok[:], [128, TC, E], BF16)
                    emit_ffn(lw, t_wT)
                    plan.fence()
                    dump_tensor(f"ffn1_{d}", x_res[:], [128, KC, T])
                    with nc.Block(f"ffn1_{d}") as blk:
                        plan.flush(blk)
                with ExitStack() as actx:
                    att = dict(
                        A=actx.enter_context(
                            nc.sbuf_tensor("A_T", [128, KC, T], BF16)),
                        qT=actx.enter_context(
                            nc.sbuf_tensor("qT", [128, T], BF16)),
                        qkv_sb=actx.enter_context(
                            nc.sbuf_tensor("qkv_sb", [128, T], BF16)),
                        kT=actx.enter_context(
                            nc.sbuf_tensor("kT", [128, T], BF16)),
                        vT=actx.enter_context(
                            nc.sbuf_tensor("vT", [128, T], BF16)),
                        Qp=actx.enter_context(
                            nc.sbuf_tensor("Qp", [128, 2, T], BF16)),
                        Kp=actx.enter_context(
                            nc.sbuf_tensor("Kp", [128, TC, NF], BF16)),
                        Vx=actx.enter_context(
                            nc.sbuf_tensor("Vx", [128, TC, 65], BF16)),
                        kvx=actx.enter_context(
                            nc.sbuf_tensor("kvx", [128, 2, 65], BF16)),
                        pT=actx.enter_context(
                            nc.sbuf_tensor("pT", [128, 2, 128], BF16)),
                        ph=actx.enter_context(
                            nc.sbuf_tensor("ph", [128, 2, 2, 64], BF16)),
                        wqh=actx.enter_context(
                            nc.sbuf_tensor("wqh", [128, 2, KC, 64], BF16)),
                        wkh=actx.enter_context(
                            nc.sbuf_tensor("wkh", [128, 2, KC, 64], BF16)),
                        wvh=actx.enter_context(
                            nc.sbuf_tensor("wvh", [128, 2, KC, 64], BF16)),
                        woc=actx.enter_context(
                            nc.sbuf_tensor("woc", [128, 2, KC, 128], BF16)),
                    )
                    t_ln = emit_ln()
                    dump_tensor(f"ln2_{d}", x_ln[:], [128, KC, T], BF16)
                    emit_attn(lw, t_ln)
                    plan.fence()
                    dump_tensor(f"A_{d}", att["A"][:], [128, KC, T], BF16)
                    dump_tensor(f"attn_{d}", x_res[:], [128, KC, T])
                    with nc.Block(f"attn_{d}") as blk:
                        plan.flush(blk)
                with ExitStack() as fctx:
                    ffn = dict(
                        h_sb=fctx.enter_context(
                            nc.sbuf_tensor("h_sb2", [128, 16, T], BF16)),
                        w1s=fctx.enter_context(
                            nc.sbuf_tensor("w1s2", [128, 2, KC, 128], BF16)),
                        w3s=fctx.enter_context(
                            nc.sbuf_tensor("w3s2", [128, 2, KC, 128], BF16)),
                        w2s=fctx.enter_context(
                            nc.sbuf_tensor("w2s2", [128, 2, 16, 128], BF16)),
                        wb=fctx.enter_context(
                            nc.sbuf_tensor("wb2", [128, T], BF16)),
                    )
                    t_ln = emit_ln()
                    t_wT = emit_routing(t_ln)
                    emit_ffn(lw, t_wT)
                    plan.fence()
                    dump_tensor(f"ffn2_{d}", x_res[:], [128, KC, T])
                    with nc.Block(f"ffn2_{d}") as blk:
                        plan.flush(blk)

            ev_tok = [None, None, None, None]
            tr_tok = [None, None, None, None]
            t_out = None
            for tc in range(TC):
                for c in range(KC):
                    b = c % 4
                    tr_tok[b] = plan.emit("tensor", TR(
                        A_(0, 128, b * 512, b * 512 + 128),
                        x_res[:, c, tc * 128:(tc + 1) * 128], idf[:]),
                        waits=[ev_tok[b]], inc="T")
                    ev_tok[b] = plan.emit("vector", CP(
                        xio[:, c * 128:(c + 1) * 128],
                        A_(0, 128, b * 512, b * 512 + 128)),
                        waits=[tr_tok[b], t_out if c == 0 else None], inc="V")
                t_out = plan.emit("sync", DMA(
                    out_hbm[tc * 128:(tc + 1) * 128, :], xio),
                    waits=list(ev_tok), inc="IO", amt=16)
            final_io = plan.cnt["IO"]
            plan.emit("sync", lambda e: e.wait_ge(plan.sems["IO"], final_io))
            with nc.Block("out") as blk:
                plan.flush(blk)

        plan.check()
    return dumps


def _run(inputs, dump=(), core_ids=None, trace=False):
    nc = bass.Bass("TRN2", target_bir_lowering=False,
                   detect_race_conditions=False)
    build_program(nc, dump=dump)
    x = np.ascontiguousarray(inputs["L0_attn_x"], dtype=np.float32)
    base = {}
    for d in range(DEPTH):
        p = f"L{d}_"
        base[p + "Wq"] = inputs[p + "attn_Wq"]
        base[p + "Wk"] = inputs[p + "attn_Wk"]
        base[p + "Wv"] = inputs[p + "attn_Wv"]
        base[p + "Wo"] = inputs[p + "attn_Wo"]
        base[p + "bq"] = inputs[p + "attn_bq"]
        base[p + "bk"] = inputs[p + "attn_bk"]
        base[p + "bv"] = inputs[p + "attn_bv"]
        base[p + "bo"] = inputs[p + "attn_bo"]
        base[p + "proj"] = inputs[p + "attn_projection"]
        base[p + "gate"] = inputs[p + "ffn_gate_w"]
        base[p + "w1"] = inputs[p + "ffn_w1"]
        base[p + "w2"] = inputs[p + "ffn_w2"]
        base[p + "w3"] = inputs[p + "ffn_w3"]
    base = {k: np.ascontiguousarray(v, dtype=np.float32)
            for k, v in base.items()}
    if core_ids is None:
        core_ids = list(range(8))
    in_maps = []
    for b in core_ids:
        m = dict(base)
        m["x"] = np.ascontiguousarray(x[b])
        in_maps.append(m)
    res = run_bass_kernel_spmd(nc, in_maps, core_ids=core_ids, trace=trace)
    out = np.stack([np.asarray(r["out"]) for r in res.results],
                   axis=0).astype(np.float32)
    extras = None
    if dump:
        extras = [{k: np.asarray(r["dump_" + k]) for k in dump}
                  for r in res.results]
    return out, extras, res


def kernel(**inputs):
    out, _, _ = _run(inputs)
    return out


# version 07
# speedup vs baseline: 1.1155x; 1.1155x over previous
import numpy as np
import concourse.bass as bass
import concourse.mybir as mybir
from concourse.bass_utils import run_bass_kernel_spmd
from contextlib import ExitStack

import concourse.bass_utils as _bu


def _patch_ldw():
    orig = _bu.bir_verify_and_optimise
    if getattr(_bu, "_ldw_patched", False):
        return
    import functools

    @functools.wraps(orig)
    def wrapper(*a, **kw):
        import subprocess
        orig_run = _bu.run_command

        def run2(cmd, **k):
            cmd = [c.replace("--enable-ldw-opt=false", "--enable-ldw-opt=true")
                   if isinstance(c, str) else c for c in cmd]
            return orig_run(cmd, **k)

        _bu.run_command = run2
        try:
            return orig(*a, **kw)
        finally:
            _bu.run_command = orig_run

    _bu.bir_verify_and_optimise = wrapper
    _bu._ldw_patched = True


_patch_ldw()

F32 = mybir.dt.float32
BF16 = mybir.dt.bfloat16
AF = mybir.ActivationFunctionType
ALU = mybir.AluOpType
AX = mybir.AxisListType

T = 2048
D = 1024
E = 8
FF = 4096
NH = 16
HD = 64
NF = 256
KC = D // 128
NT = T // 512
TC = T // 128
DEPTH = 1

ENGINES = ("sync", "tensor", "vector", "scalar", "gpsimd")


class Plan:

    def __init__(self, nc, strict=False):
        self.nc = nc
        self.strict = strict
        self.sems = {}
        self.owner = {}
        self.cnt = {}
        self.hw = {e: {} for e in ENGINES}
        self.ops = []
        self.fence_pending = {}

    def sem(self, name, owner):
        self.sems[name] = self.nc.alloc_semaphore(name)
        self.owner[name] = owner
        self.cnt[name] = 0

    def emit(self, engine, fn, waits=(), inc=None, amt=1):
        w = {}
        for t in waits:
            if t is None:
                continue
            s, v = t
            if v <= 0:
                continue
            if not self.strict and self.owner.get(s) == engine:
                continue
            w[s] = max(w.get(s, 0), v)
        for t in self.fence_pending.pop(engine, ()):
            s, v = t
            if v <= 0 or (not self.strict and self.owner.get(s) == engine):
                continue
            w[s] = max(w.get(s, 0), v)
        if self.strict:
            own = {"tensor": "T", "vector": "V", "scalar": "S",
                   "gpsimd": "G"}.get(engine)
            if own is not None:
                w[own] = max(w.get(own, 0), self.cnt[own])
                if inc is None:
                    inc = own
        w2 = []
        for s, v in sorted(w.items()):
            if self.hw[engine].get(s, 0) >= v:
                continue
            self.hw[engine][s] = v
            w2.append((s, v))
        tok = None
        if inc is not None:
            self.cnt[inc] += amt
            tok = (inc, self.cnt[inc])
        self.ops.append((engine, fn, tuple(w2), (inc, amt) if inc else None))
        return tok

    def barrier_self(self, engine, tok):
        s, v = tok
        h = self.sems[s]
        self.ops.append((engine, lambda e, h=h, v=v: e.wait_ge(h, v), (), None))

    def fence(self):
        snap = [(s, c) for s, c in self.cnt.items() if c > 0]
        for e in ENGINES:
            self.fence_pending[e] = list(snap)

    def flush(self, block):
        ops = self.ops
        self.ops = []
        for ename in ENGINES:
            eops = [o for o in ops if o[0] == ename]
            if not eops:
                continue

            def body(eng, eops=eops):
                for (_, fn, waits, inc) in eops:
                    for (s, v) in waits:
                        eng.wait_ge(self.sems[s], v)
                    ins = fn(eng)
                    if inc is not None:
                        assert ins is not None
                        ins.then_inc(self.sems[inc[0]], inc[1])

            getattr(block, ename)(body)

    def check(self):
        assert not self.ops, "unflushed ops"
        for e, hwm in self.hw.items():
            for s, v in hwm.items():
                assert v <= self.cnt[s], (e, s, v, self.cnt[s])


def _r64(h):
    half = h % 2
    return slice(64 * half, 64 * half + 64)


def build_program(nc, dump=(), strict=False):
    dumps = {}

    def din(name, shape):
        return nc.dram_tensor(name, shape, F32, kind="ExternalInput").ap()

    x_in = din("x", [T, D])
    out_hbm = nc.dram_tensor("out", [T, D], F32, kind="ExternalOutput").ap()
    L = []
    for d in range(DEPTH):
        L.append(dict(
            Wq=din(f"L{d}_Wq", [D, D]), Wk=din(f"L{d}_Wk", [D, D]),
            Wv=din(f"L{d}_Wv", [D, D]), Wo=din(f"L{d}_Wo", [D, D]),
            bq=din(f"L{d}_bq", [D]), bk=din(f"L{d}_bk", [D]),
            bv=din(f"L{d}_bv", [D]), bo=din(f"L{d}_bo", [D]),
            proj=din(f"L{d}_proj", [NH, NF, HD]),
            gate=din(f"L{d}_gate", [D, E]),
            w1=din(f"L{d}_w1", [E, D, FF]),
            w2=din(f"L{d}_w2", [E, FF, D]),
            w3=din(f"L{d}_w3", [E, D, FF]),
        ))

    plan = Plan(nc, strict=strict)
    plan.sem("WL", None)
    plan.sem("IO", None)
    plan.sem("T", "tensor")
    plan.sem("V", "vector")
    plan.sem("S", "scalar")
    plan.sem("G", "gpsimd")

    with ExitStack() as ctx:
        def sb(name, shape, dt):
            return ctx.enter_context(nc.sbuf_tensor(name, shape, dt))

        x_res = sb("x_res", [128, KC, T], F32)
        x_ln = sb("x_ln", [128, KC, T], BF16)
        scr = sb("scr", [128, T], F32)
        rowbuf = sb("rowbuf", [128, T], BF16)
        ones128 = sb("ones128", [128, 128], BF16)
        idb = sb("idb", [128, 128], BF16)
        idf = sb("idf", [128, 128], F32)
        iot8 = sb("iot8", [8, 128], F32)
        selE = sb("selE", [8, 128], BF16)
        iorow = sb("iorow", [128, 128], F32)
        pcol = sb("pcol", [128, 1], F32)
        bqf = sb("bqf", [128, KC], F32)
        bkf = sb("bkf", [128, KC], F32)
        bvf = sb("bvf", [128, KC], F32)
        bof = sb("bof", [128, KC], F32)
        gate_bf = sb("gate_bf", [128, KC, E], BF16)
        logits = sb("logits", [128, TC, E], F32)
        wtok = sb("wtok", [128, TC, E], BF16)
        rt = sb("rt", [128, 6, TC, E], F32)
        rc = sb("rc", [128, 6, TC, 1], F32)
        wTr = sb("wTr", [8, TC, 128], BF16)
        xio = scr[:, 0:1024]
        arena = ctx.enter_context(nc.psum_tensor("arena", [128, 4096], F32))

        def A_(p0, p1, c0, c1):
            return arena[p0:p1, c0:c1]

        def Abf(p0, p1, c0, c1):
            return arena[p0:p1, c0:c1].bitcast(BF16)

        def MM(o, l, r, st, sp):
            return lambda e: e.matmul(o, l, r, start=st, stop=sp,
                                      skip_group_check=True)

        def TR(o, i, ident):
            return lambda e: e.transpose(o, i, ident)

        def TT(o, a, b, op):
            return lambda e: e.tensor_tensor(o, a, b, op)

        def STT(o, a, s, b, op0, op1):
            return lambda e: e.scalar_tensor_tensor(o, a, s, b, op0, op1)

        def TS(o, a, s1, s2, op0, op1=ALU.bypass):
            return lambda e: e.tensor_scalar(o, a, s1, s2, op0, op1)

        def CP(o, i):
            return lambda e: e.tensor_copy(o, i)

        def ACT(o, i, f, bias=0.0, scale=1.0):
            return lambda e: e.activation(o, i, f, bias=bias, scale=scale)

        def RED(o, i, op):
            return lambda e: e.tensor_reduce(o, i, AX.X, op)

        def RECIP(o, i):
            return lambda e: e.reciprocal(o, i)

        def DMA(o, i, slow=False):
            if slow:
                return lambda e: e.dma_start(out=o, in_=i,
                                             allow_slow_non_contiguous=True)
            return lambda e: e.dma_start(out=o, in_=i)

        def dump_tensor(tag, src_ap, shape, dt=F32):
            if tag not in dump:
                return
            dptr = nc.dram_tensor("dump_" + tag, shape, dt,
                                  kind="ExternalOutput").ap()
            dumps[tag] = dptr
            plan.fence()
            idx = tuple(slice(None) for _ in shape)
            plan.emit("sync", DMA(dptr[idx], src_ap), inc="IO", amt=16)
            plan.fence()

        tg_iot = plan.emit("gpsimd", lambda e: e.iota(
            iot8[:], [[0, 128]], channel_multiplier=1,
            allow_small_or_imprecise_dtypes=True), inc="G")
        tg_row = plan.emit("gpsimd", lambda e: e.iota(
            iorow[:], [[1, 128]], channel_multiplier=0,
            allow_small_or_imprecise_dtypes=True), inc="G")
        tg_col = plan.emit("gpsimd", lambda e: e.iota(
            pcol[:], [[0, 1]], channel_multiplier=1,
            allow_small_or_imprecise_dtypes=True), inc="G")
        tg = plan.emit("gpsimd", lambda e: e.memset(ones128[:], 1.0), inc="G")
        plan.emit("vector", TS(idf[:], iorow[:], pcol[:], None, ALU.is_equal),
                  waits=[tg_row, tg_col], inc="V")
        tid = plan.emit("vector", CP(idb[:], idf[:]), inc="V")

        lw0 = L[0]
        for (dst, src) in ((bqf, lw0["bq"]), (bkf, lw0["bk"]),
                           (bvf, lw0["bv"]), (bof, lw0["bo"])):
            plan.emit("sync", DMA(dst[:], src.rearrange("(c p) -> p c", p=128),
                                  slow=True), inc="IO", amt=16)
        t_gate = plan.emit("gpsimd", DMA(
            gate_bf[:], lw0["gate"].rearrange("(c k) e -> k c e", k=128)),
            inc="WL", amt=16)

        ev_tok = [None, None, None, None]
        tr_tok = [None, None, None, None]
        for tc in range(TC):
            t_in = plan.emit("sync", DMA(xio, x_in[tc * 128:(tc + 1) * 128, :]),
                             waits=list(tr_tok), inc="IO", amt=16)
            for c in range(KC):
                b = c % 4
                tr_tok[b] = plan.emit("tensor", TR(
                    A_(0, 128, b * 512, b * 512 + 128),
                    xio[:, c * 128:(c + 1) * 128], idf[:]),
                    waits=[t_in, tid, ev_tok[b]], inc="T")
                ev_tok[b] = plan.emit("vector", CP(
                    x_res[:, c, tc * 128:(tc + 1) * 128],
                    A_(0, 128, b * 512, b * 512 + 128)),
                    waits=[tr_tok[b]], inc="V")
        plan.fence()

        def emit_ln():
            sq = (scr[:, 0:1024].bitcast(BF16), scr[:, 1024:2048].bitcast(BF16))
            t_sq_cons = [None, None]
            t_xbf = [None] * KC
            t_sq = [None] * KC
            t_st = None
            for c in range(KC):
                t_xbf[c] = plan.emit("scalar", ACT(x_ln[:, c, :], x_res[:, c, :],
                                                   AF.Copy), inc="S")
                t_sq[c] = plan.emit("scalar", ACT(sq[c % 2], x_res[:, c, :],
                                                  AF.Square),
                                    waits=[t_sq_cons[c % 2]], inc="S")
                for nt in range(NT):
                    plan.emit("tensor", MM(
                        A_(0, 1, nt * 512, (nt + 1) * 512), ones128[:, 0:1],
                        x_ln[:, c, nt * 512:(nt + 1) * 512],
                        c == 0, c == KC - 1), waits=[t_xbf[c]])
                tok = None
                for nt in range(NT):
                    tok = plan.emit("tensor", MM(
                        A_(0, 1, 2048 + nt * 512, 2048 + (nt + 1) * 512),
                        ones128[:, 0:1],
                        sq[c % 2][:, nt * 512:(nt + 1) * 512],
                        c == 0, c == KC - 1),
                        waits=[t_sq[c]], inc="T" if nt == NT - 1 else None)
                t_sq_cons[c % 2] = tok
                t_st = tok
            var = scr[0:1, 0:2048]
            row = rowbuf[0:1, :]
            tva = plan.emit("vector", TS(var, A_(0, 1, 2048, 4096), 1.0 / D,
                                         None, ALU.mult), waits=[t_st], inc="V")
            tmb = plan.emit("vector", TS(row, A_(0, 1, 0, 2048), 1.0 / D,
                                         None, ALU.mult), waits=[t_st], inc="V")
            tb = []
            tbm = None
            for nt in range(NT):
                tbm = plan.emit("tensor", MM(
                    A_(0, 128, nt * 512, (nt + 1) * 512), ones128[0:1, :],
                    row[:, nt * 512:(nt + 1) * 512], True, True),
                    waits=[tmb, tva], inc="T")
                tb.append(tbm)
            tms = plan.emit("scalar", ACT(row, row, AF.Square), waits=[tbm],
                            inc="S")
            tv2 = plan.emit("vector", TT(var, var, row, ALU.subtract),
                            waits=[tms], inc="V")
            tv3 = plan.emit("vector", TS(var, var, 1e-5, None, ALU.add),
                            inc="V")
            plan.emit("scalar", ACT(var, var, AF.Ln), waits=[tv3], inc="S")
            trs = plan.emit("scalar", ACT(row, var, AF.Exp, scale=-0.5),
                            inc="S")
            for nt in range(NT):
                tb.append(plan.emit("tensor", MM(
                    A_(0, 128, 2048 + nt * 512, 2048 + (nt + 1) * 512),
                    ones128[0:1, :], row[:, nt * 512:(nt + 1) * 512],
                    True, True), waits=[trs], inc="T"))
            t_last = None
            for c in range(KC):
                plan.emit("vector", TT(scr[:, :], x_res[:, c, :],
                                       A_(0, 128, 0, 2048), ALU.subtract),
                          waits=tb, inc="V")
                t_last = plan.emit("vector", TT(x_ln[:, c, :], scr[:, :],
                                                A_(0, 128, 2048, 4096),
                                                ALU.mult), inc="V")
            return t_last

        def emit_routing(t_ln):
            tev = [None, None]
            for tc in range(TC):
                slot = A_(0, 128, (tc % 2) * 512, (tc % 2) * 512 + E)
                tok = None
                for kc in range(KC):
                    tok = plan.emit("tensor", MM(
                        slot, x_ln[:, kc, tc * 128:(tc + 1) * 128],
                        gate_bf[:, kc, :], kc == 0, kc == KC - 1),
                        waits=[t_ln, t_gate, tev[tc % 2]],
                        inc="T" if kc == KC - 1 else None)
                tev[tc % 2] = plan.emit("vector", CP(logits[:, tc, :], slot),
                                        waits=[tok], inc="V")

            lg = logits[:]
            b = lambda i: rc[:, i, :, :].broadcast_to([128, TC, E])
            r_ = lambda i: rt[:, i, :, :]
            c_ = lambda i: rc[:, i, :, :]
            def V(fn, waits=(), bar=False):
                tok = plan.emit("vector", fn, waits=waits, inc="V")
                if bar:
                    plan.barrier_self("vector", tok)
                return tok
            S = lambda fn, waits=(): plan.emit("scalar", fn, waits=waits,
                                               inc="S")
            V(RED(c_(0), lg, ALU.max), bar=True)
            V(TT(r_(0), lg, b(0), ALU.is_equal))
            V(TS(r_(1), r_(0), 1e30, None, ALU.mult))
            V(TT(r_(1), lg, r_(1), ALU.subtract))
            V(RED(c_(1), r_(1), ALU.max), bar=True)
            tsa = S(ACT(r_(2), lg, AF.Abs), waits=[tev[0], tev[1]])
            V(TT(r_(3), r_(2), b(0), ALU.max), waits=[tsa])
            V(TT(r_(4), b(0), lg, ALU.subtract))
            tv = V(STT(r_(3), r_(3), 0.02, r_(4), ALU.mult, ALU.subtract))
            V(TS(r_(3), r_(3), 0.0, None, ALU.is_ge))
            ts1 = S(ACT(r_(4), r_(4), AF.Exp, scale=-1.0), waits=[tv])
            V(TT(r_(4), r_(3), r_(4), ALU.mult), waits=[ts1])
            V(RED(c_(4), r_(4), ALU.add), bar=True)
            V(RECIP(c_(2), c_(4)), bar=True)
            V(TT(r_(3), r_(2), b(1), ALU.max))
            tv2 = V(TT(r_(5), b(1), lg, ALU.subtract))
            V(STT(r_(3), r_(3), 0.02, r_(5), ALU.mult, ALU.subtract))
            V(TS(r_(3), r_(3), 0.0, None, ALU.is_ge))
            V(TT(r_(2), r_(3), r_(0), ALU.mult))
            tv3 = V(TT(r_(3), r_(3), r_(2), ALU.subtract))
            ts2 = S(ACT(r_(5), r_(5), AF.Exp, scale=-1.0), waits=[tv3])
            V(TT(r_(5), r_(3), r_(5), ALU.mult), waits=[ts2])
            V(RED(c_(5), r_(5), ALU.add), bar=True)
            V(RECIP(c_(3), c_(5)), bar=True)
            V(TT(r_(1), r_(1), b(1), ALU.is_equal))
            V(TT(r_(0), r_(0), b(2), ALU.mult))
            V(TT(r_(1), r_(1), b(3), ALU.mult))
            t_w = V(TT(wtok[:], r_(0), r_(1), ALU.add))
            tevw = [None, None]
            t_wT = None
            for tc in range(TC):
                col = 3072 + (tc % 2) * 512
                slot = Abf(0, 8, col, col + 64)
                tok = plan.emit("tensor", TR(slot, wtok[:, tc, :], idb[:]),
                                waits=[t_w, tid, tevw[tc % 2]], inc="T")
                tevw[tc % 2] = t_wT = plan.emit(
                    "vector", CP(wTr[:, tc, :], slot), waits=[tok], inc="V")
            return t_wT

        def emit_ffn(lw, t_wT):
            h_sb = ffn["h_sb"]
            w1s, w3s, w2s = ffn["w1s"], ffn["w3s"], ffn["w2s"]
            wb = ffn["wb"]
            wT_flat = wTr[:].rearrange("p a b -> p (a b)")
            t_hev = [None] * 16
            t_silu = [None] * 16
            st = dict(h1_last=None, h3_last=None, y_last=None,
                      yev_b03=None, yev_b47=None, ymm_last=None,
                      yev_last=None, sel_used=None)
            for e_ in range(E):
                t_sel = plan.emit("vector", TS(selE[:], iot8[:], float(e_),
                                               None, ALU.is_equal),
                                  waits=[st["sel_used"], tg_iot], inc="V")
                twb_mm = None
                for nt in range(NT):
                    twb_mm = plan.emit("tensor", MM(
                        A_(0, 128, 2048 + nt * 512, 2048 + (nt + 1) * 512),
                        selE[:],
                        wT_flat[:, nt * 512:(nt + 1) * 512], True, True),
                        waits=[t_wT, t_sel, t_hev[15], st["yev_b47"]],
                        inc="T" if nt == NT - 1 else None)
                st["sel_used"] = twb_mm
                t_wb = plan.emit("vector", TS(wb[:], A_(0, 128, 2048, 4096),
                                              0.5, None, ALU.mult),
                                 waits=[twb_mm], inc="V")
                for fc in range(2):
                    t_w1 = [None] * 16
                    t_w3 = [None] * 16
                    t_h1 = [None] * 16
                    t_h3 = [None] * 16
                    for fs in range(16):
                        off = fc * 2048 + fs * 128
                        t_w1[fs] = plan.emit("gpsimd", DMA(
                            w1s[:, fs % 2, :, :],
                            lw["w1"][e_, :, off:off + 128].rearrange(
                                "(kc k) f -> k kc f", k=128)),
                            waits=[t_h1[fs - 2] if fs >= 2 else st["h1_last"]],
                            inc="WL", amt=16)
                        t_w3[fs] = plan.emit("gpsimd", DMA(
                            w3s[:, fs % 2, :, :],
                            lw["w3"][e_, :, off:off + 128].rearrange(
                                "(kc k) f -> k kc f", k=128)),
                            waits=[t_h3[fs - 2] if fs >= 2 else st["h3_last"]],
                            inc="WL", amt=16)
                        tok = None
                        for kc in range(KC):
                            for nt in range(NT):
                                tok = plan.emit("tensor", MM(
                                    A_(0, 128, nt * 512, (nt + 1) * 512),
                                    w1s[:, fs % 2, kc, :],
                                    x_ln[:, kc, nt * 512:(nt + 1) * 512],
                                    kc == 0, kc == KC - 1),
                                    waits=[t_w1[fs],
                                           t_silu[fs - 1] if fs else t_silu[15],
                                           st["yev_b03"] if fs == 0 else None,
                                           t_wT if (e_ == 0 and fc == 0
                                                    and fs == 0) else None],
                                    inc="T" if (kc == KC - 1 and nt == NT - 1)
                                    else None)
                        t_h1[fs] = tok
                        tok = None
                        for kc in range(KC):
                            for nt in range(NT):
                                tok = plan.emit("tensor", MM(
                                    A_(0, 128, 2048 + nt * 512,
                                       2048 + (nt + 1) * 512),
                                    w3s[:, fs % 2, kc, :],
                                    x_ln[:, kc, nt * 512:(nt + 1) * 512],
                                    kc == 0, kc == KC - 1),
                                    waits=[t_w3[fs],
                                           t_wb if fs == 0 else None,
                                           t_hev[fs - 1] if fs else t_hev[15],
                                           st["yev_b47"] if fs == 0 else None],
                                    inc="T" if (kc == KC - 1 and nt == NT - 1)
                                    else None)
                        t_h3[fs] = tok
                        t_silu[fs] = plan.emit("scalar", ACT(
                            h_sb[:, fs, :], A_(0, 128, 0, 2048), AF.Silu),
                            waits=[t_h1[fs],
                                   st["ymm_last"] if fs == 0 else None],
                            inc="S")
                        t_hev[fs] = plan.emit("vector", TT(
                            h_sb[:, fs, :], h_sb[:, fs, :],
                            A_(0, 128, 2048, 4096), ALU.mult),
                            waits=[t_silu[fs], t_h3[fs]],
                            inc="V")
                    st["h1_last"] = t_h1[14]
                    st["h3_last"] = t_h3[14]
                    ymm = [None] * KC
                    yev = [None] * KC
                    t_w2 = [None] * KC
                    for hoc in range(KC):
                        t_w2[hoc] = plan.emit("gpsimd", DMA(
                            w2s[:, hoc % 2, :, :],
                            lw["w2"][e_, fc * 2048:(fc + 1) * 2048,
                                     hoc * 128:(hoc + 1) * 128].rearrange(
                                "(fs k) h -> k fs h", k=128)),
                            waits=[ymm[hoc - 2] if hoc >= 2 else st["y_last"]],
                            inc="WL", amt=16)
                        base = (hoc % 2) * 2048
                        tok = None
                        for fs in range(16):
                            for nt in range(NT):
                                tok = plan.emit("tensor", MM(
                                    A_(0, 128, base + nt * 512,
                                       base + (nt + 1) * 512),
                                    w2s[:, hoc % 2, fs, :],
                                    h_sb[:, fs, nt * 512:(nt + 1) * 512],
                                    fs == 0, fs == 15),
                                    waits=[t_w2[hoc], t_hev[15],
                                           yev[hoc - 2] if hoc >= 2 else None,
                                           t_silu[15] if hoc == 0 else None],
                                    inc="T" if (fs == 15 and nt == NT - 1)
                                    else None)
                        ymm[hoc] = tok
                        plan.emit("vector", TT(
                            scr[:, :], A_(0, 128, base, base + 2048), wb[:],
                            ALU.mult), waits=[tok, t_wb], inc="V")
                        yev[hoc] = plan.emit("vector", TT(
                            x_res[:, hoc, :], x_res[:, hoc, :], scr[:, :],
                            ALU.add), inc="V")
                    st["y_last"] = ymm[KC - 1]
                    st["ymm_last"] = ymm[KC - 1]
                    st["yev_b03"] = yev[KC - 2]
                    st["yev_b47"] = yev[KC - 1]
                    st["yev_last"] = yev[KC - 1]
            return st["yev_last"]

        def emit_attn(lw, t_ln):
            A_T = att["A"]
            qT, kT, vT = att["qT"], att["kT"], att["vT"]
            qkv_sb = att["qkv_sb"]
            Qp, Kp, Vx, kvx = att["Qp"], att["Kp"], att["Vx"], att["kvx"]
            pT = att["pT"]
            wqh, wkh, wvh, ph = att["wqh"], att["wkh"], att["wvh"], att["ph"]
            woc = att["woc"]
            tg1 = plan.emit("gpsimd", lambda e: e.memset(Vx[:, :, 64:65], 1.0),
                            inc="G")
            H = [dict() for _ in range(NH)]

            def hd(i, key, default=None):
                if i < 0:
                    return default
                return H[i].get(key, default)

            def proj(h):
                hb = h % 2
                c = h // 2
                r = _r64(h)
                s = H[h]
                tq = plan.emit("gpsimd", DMA(wqh[:, hb, :, :], lw["Wq"][
                    :, h * 64:(h + 1) * 64].rearrange("(kc k) m -> k kc m",
                                                      k=128)),
                    waits=[hd(h - 2, "t_qmm")], inc="WL", amt=16)
                tk = plan.emit("gpsimd", DMA(wkh[:, hb, :, :], lw["Wk"][
                    :, h * 64:(h + 1) * 64].rearrange("(kc k) m -> k kc m",
                                                      k=128)),
                    waits=[hd(h - 2, "t_kmm")], inc="WL", amt=16)
                tv = plan.emit("gpsimd", DMA(wvh[:, hb, :, :], lw["Wv"][
                    :, h * 64:(h + 1) * 64].rearrange("(kc k) m -> k kc m",
                                                      k=128)),
                    waits=[hd(h - 2, "t_vmm")], inc="WL", amt=16)
                tp = plan.emit("gpsimd", DMA(ph[:, hb, :, :], lw["proj"][
                    h].rearrange("(pc k) d -> k pc d", k=128)),
                    waits=[hd(h - 2, "t_ptr")], inc="WL", amt=16)
                prev_b03 = hd(h - 1, "t_vev", t_ln if h <= 1 else None)
                t_ptev = None
                for pc in range(2):
                    slot = Abf(r.start, r.start + 64, pc * 512, pc * 512 + 64)
                    tok = plan.emit("tensor", TR(slot, ph[:, hb, pc, :],
                                                 idb[:]),
                                    waits=[tp, tid, prev_b03,
                                           hd(h - 2, "t_kpmm")], inc="T")
                    s["t_ptr"] = tok
                    t_ptev = plan.emit("vector", CP(pT[r, pc, :], slot),
                                       waits=[tok], inc="V")
                s["t_ptev"] = t_ptev
                yield
                tok = None
                for kc in range(KC):
                    for nt in range(NT):
                        tok = plan.emit("tensor", MM(
                            A_(r.start, r.stop, nt * 512, (nt + 1) * 512),
                            wqh[:, hb, kc, :],
                            x_ln[:, kc, nt * 512:(nt + 1) * 512],
                            kc == 0, kc == KC - 1),
                            waits=[tq, t_ptev],
                            inc="T" if (kc == KC - 1 and nt == NT - 1) else None)
                s["t_qmm"] = tok
                s["t_qev"] = plan.emit("vector", TS(
                    qT[r, :], A_(r.start, r.stop, 0, 2048),
                    bqf[r, c:c + 1], None, ALU.add),
                    waits=[tok, hd(h - 2, "t_qp1")], inc="V")
                yield
                tok = None
                for kc in range(KC):
                    for nt in range(NT):
                        tok = plan.emit("tensor", MM(
                            A_(r.start, r.stop, nt * 512, (nt + 1) * 512),
                            wkh[:, hb, kc, :],
                            x_ln[:, kc, nt * 512:(nt + 1) * 512],
                            kc == 0, kc == KC - 1),
                            waits=[tk, s["t_qev"]],
                            inc="T" if (kc == KC - 1 and nt == NT - 1) else None)
                s["t_kmm"] = tok
                s["t_kev"] = plan.emit("scalar", ACT(
                    kT[r, :], A_(r.start, r.stop, 0, 2048), AF.Identity,
                    bias=bkf[r, c:c + 1]),
                    waits=[tok, hd(h - 2, "t_kpmm")], inc="S")
                yield
                tok = None
                for kc in range(KC):
                    for nt in range(NT):
                        tok = plan.emit("tensor", MM(
                            A_(r.start, r.stop, nt * 512, (nt + 1) * 512),
                            wvh[:, hb, kc, :],
                            x_ln[:, kc, nt * 512:(nt + 1) * 512],
                            kc == 0, kc == KC - 1),
                            waits=[tv, s["t_kev"]],
                            inc="T" if (kc == KC - 1 and nt == NT - 1) else None)
                s["t_vmm"] = tok
                s["t_vev"] = plan.emit("scalar", ACT(
                    vT[r, :], A_(r.start, r.stop, 0, 2048), AF.Identity,
                    bias=bvf[r, c:c + 1]),
                    waits=[tok, hd(h - 2, "t_vtmm")], inc="S")
                yield

            def chain(h):
                c = h // 2
                r = _r64(h)
                s = H[h]
                C = 2048

                def norm2(src, t_src, t_row_free, t_banks_free, extra_sq_wait):
                    sqv = scr[r, 0:1024].bitcast(BF16)
                    tsq = plan.emit("scalar", ACT(sqv, src, AF.Square),
                                    waits=[t_src, t_row_free, extra_sq_wait],
                                    inc="S")
                    tok = None
                    for nt in range(NT):
                        tok = plan.emit("tensor", MM(
                            A_(0, 1, C + nt * 512, C + (nt + 1) * 512),
                            ones128[r, 0:1],
                            sqv[:, nt * 512:(nt + 1) * 512], True, True),
                            waits=[tsq, t_banks_free],
                            inc="T" if nt == NT - 1 else None)
                    t_st = tok
                    rrow = scr[0:1, 0:2048]
                    t1 = plan.emit("scalar", ACT(rrow, A_(0, 1, C, C + 2048),
                                                 AF.Ln), waits=[tok], inc="S")
                    plan.emit("scalar", ACT(rrow, rrow, AF.Exp, scale=0.5),
                              inc="S")
                    t3 = plan.emit("vector", TS(rrow, rrow, 1e-8, None,
                                                ALU.add),
                                   waits=[("S", plan.cnt["S"])], inc="V")
                    plan.emit("scalar", ACT(rrow, rrow, AF.Ln), waits=[t3],
                              inc="S")
                    inv = rowbuf[0:1, :]
                    t4 = plan.emit("scalar", ACT(inv, rrow, AF.Exp,
                                                 scale=-1.0), inc="S")
                    tokb = None
                    for nt in range(NT):
                        tokb = plan.emit("tensor", MM(
                            A_(r.start, r.stop, C + nt * 512,
                               C + (nt + 1) * 512),
                            ones128[0:1, 0:64],
                            inv[:, nt * 512:(nt + 1) * 512], True, True),
                            waits=[t4, t1], inc="T" if nt == NT - 1 else None)
                    tn = plan.emit("vector", TT(
                        src, src, A_(r.start, r.stop, C, C + 2048), ALU.mult),
                        waits=[tokb], inc="V")
                    return tn, t_st

                t_qn, t_stq = norm2(qT[r, :], s["t_qev"],
                                    hd(h - 1, "t_nrm_done"),
                                    hd(h - 1, "t_Aev"),
                                    hd(h - 2, "t_stk"))
                yield
                t_kn, t_stk = norm2(kT[r, :], s["t_kev"], t_qn, t_qn, t_qn)
                s["t_stk"] = t_stk
                yield
                t_qp = [None, None]
                for pc in range(2):
                    tok = None
                    for nt in range(NT):
                        tok = plan.emit("tensor", MM(
                            A_(0, 128, C + nt * 512, C + (nt + 1) * 512),
                            pT[r, pc, :],
                            qT[r, nt * 512:(nt + 1) * 512], True, True),
                            waits=[t_qn, s["t_ptev"],
                                   t_kn if pc == 0 else t_qp[0]],
                            inc="T" if nt == NT - 1 else None)
                    t_qp[pc] = plan.emit("vector", TS(
                        Qp[:, pc, :], A_(0, 128, C, C + 2048),
                        0.0, 0.1, ALU.max, ALU.add), waits=[tok], inc="V")
                    if pc == 0:
                        yield
                s["t_qp1"] = t_qp[1]
                yield
                t_kpev = [None] * TC
                tok = None
                pT_flat = pT[r, :, :].rearrange("p a b -> p (a b)")
                for tc in range(TC):
                    col = C + (tc % 2) * 512
                    tok = plan.emit("tensor", MM(
                        A_(0, 128, col, col + 256),
                        kT[r, tc * 128:(tc + 1) * 128], pT_flat, True, True),
                        waits=[t_kn, t_qp[1],
                               t_kpev[tc - 2] if tc >= 2 else None], inc="T")
                    t_kpev[tc] = plan.emit("vector", TS(
                        Kp[:, tc, :], A_(0, 128, col, col + 256),
                        0.0, 0.1, ALU.max, ALU.add), waits=[tok], inc="V")
                s["t_kpmm"] = tok
                yield
                rid = idb[r, r]
                t_vtev = [None] * TC
                tok = None
                for tc in range(TC):
                    col = C + 1024 + (tc % 2) * 512
                    slot = Abf(0, 128, col, col + 32)
                    tok = plan.emit("tensor", TR(
                        slot, vT[r, tc * 128:(tc + 1) * 128], rid),
                        waits=[s["t_vev"], t_qp[1],
                               t_vtev[tc - 2] if tc >= 2 else None], inc="T")
                    t_vtev[tc] = plan.emit("vector", CP(
                        Vx[:, tc, 0:64], slot), waits=[tok, tg1], inc="V")
                s["t_vtmm"] = tok
                yield
                t_kvev = [None, None]
                for fc in range(2):
                    col = C + fc * 512
                    tok = None
                    for tc in range(TC):
                        tok = plan.emit("tensor", MM(
                            A_(0, 128, col, col + 65),
                            Kp[:, tc, fc * 128:(fc + 1) * 128],
                            Vx[:, tc, :], tc == 0, tc == TC - 1),
                            waits=[t_kpev[TC - 1] if tc == 0 else t_kpev[tc],
                                   t_vtev[tc]],
                            inc="T" if tc == TC - 1 else None)
                    t_kvev[fc] = plan.emit("vector", CP(
                        kvx[:, fc, :], A_(0, 128, col, col + 65)),
                        waits=[tok], inc="V")
                yield
                tok = None
                for fc in range(2):
                    for nt in range(NT):
                        tok = plan.emit("tensor", MM(
                            A_(r.start, r.stop, C + nt * 512,
                               C + (nt + 1) * 512),
                            kvx[:, fc, 0:64],
                            Qp[:, fc, nt * 512:(nt + 1) * 512],
                            fc == 0, fc == 1),
                            waits=[t_kvev[fc], t_kpev[TC - 1],
                                   t_vtev[TC - 1]],
                            inc="T" if (fc == 1 and nt == NT - 1) else None)
                t_omm = tok
                t_qkv = plan.emit("scalar", ACT(
                    qkv_sb[r, :], A_(r.start, r.stop, C, C + 2048), AF.Copy),
                    waits=[t_omm], inc="S")
                yield
                tok = None
                for fc in range(2):
                    for nt in range(NT):
                        tok = plan.emit("tensor", MM(
                            A_(0, 1, C + nt * 512, C + (nt + 1) * 512),
                            kvx[:, fc, 64:65],
                            Qp[:, fc, nt * 512:(nt + 1) * 512],
                            fc == 0, fc == 1),
                            waits=[t_qkv],
                            inc="T" if (fc == 1 and nt == NT - 1) else None)
                nr = scr[0:1, 0:2048]
                tn0 = plan.emit("vector", TS(nr, A_(0, 1, C, C + 2048), 1e-6,
                                             None, ALU.max), waits=[tok],
                                inc="V")
                plan.emit("scalar", ACT(nr, nr, AF.Ln), waits=[tn0], inc="S")
                inv_n = rowbuf[0:1, :]
                tin = plan.emit("scalar", ACT(inv_n, nr, AF.Exp, scale=-1.0),
                                inc="S")
                s["t_nrm_done"] = tin
                tok = None
                for nt in range(NT):
                    tok = plan.emit("tensor", MM(
                        A_(r.start, r.stop, C + nt * 512, C + (nt + 1) * 512),
                        ones128[0:1, 0:64],
                        inv_n[:, nt * 512:(nt + 1) * 512], True, True),
                        waits=[tin], inc="T" if nt == NT - 1 else None)
                s["t_Aev"] = plan.emit("vector", TT(
                    A_T[r, c, :], qkv_sb[r, :],
                    A_(r.start, r.stop, C, C + 2048), ALU.mult),
                    waits=[t_qkv, tok], inc="V")
                yield

            def drain(g):
                if g is None:
                    return
                for _ in g:
                    pass

            def alternate(g1, g2):
                gens = [g for g in (g1, g2) if g is not None]
                while gens:
                    nxt = []
                    for g in gens:
                        try:
                            next(g)
                            nxt.append(g)
                        except StopIteration:
                            pass
                    gens = nxt

            drain(proj(0))
            drain(proj(1))
            for h in range(NH):
                g = chain(h)
                for _ in range(5):
                    next(g)
                alternate(g, proj(h + 2) if h + 2 < NH else None)

            t_Aev = H[NH - 1]["t_Aev"]
            t_woev = [None] * KC
            t_womm = [None] * KC
            for oc in range(KC):
                hb = oc % 2
                two = plan.emit("gpsimd", DMA(
                    woc[:, hb, :, :], lw["Wo"][:, oc * 128:(oc + 1) * 128]
                    .rearrange("(kc k) m -> k kc m", k=128)),
                    waits=[t_womm[oc - 2] if oc >= 2 else None],
                    inc="WL", amt=16)
                base = hb * 2048
                tok = None
                for kc in range(KC):
                    for nt in range(NT):
                        tok = plan.emit("tensor", MM(
                            A_(0, 128, base + nt * 512, base + (nt + 1) * 512),
                            woc[:, hb, kc, :],
                            A_T[:, kc, nt * 512:(nt + 1) * 512],
                            kc == 0, kc == KC - 1),
                            waits=[two, t_Aev,
                                   H[NH - 1]["t_vev"] if oc == 0 else None,
                                   t_woev[oc - 2] if oc >= 2 else None],
                            inc="T" if (kc == KC - 1 and nt == NT - 1) else None)
                t_womm[oc] = tok
                plan.emit("vector", TS(scr[:, :], A_(0, 128, base, base + 2048),
                                       bof[:, oc:oc + 1], None, ALU.add),
                          waits=[tok], inc="V")
                t_woev[oc] = plan.emit("vector", TT(
                    x_res[:, oc, :], x_res[:, oc, :], scr[:, :], ALU.add),
                    inc="V")
            return t_woev[-1]

        with nc.allow_low_precision(reason="bf16 operands are blessed for "
                                    "this problem (fp32 accumulation kept)"):
            with nc.Block("p0") as blk:
                plan.flush(blk)

            for d in range(DEPTH):
                lw = L[d]
                with ExitStack() as fctx:
                    ffn = dict(
                        h_sb=fctx.enter_context(
                            nc.sbuf_tensor("h_sb", [128, 16, T], BF16)),
                        w1s=fctx.enter_context(
                            nc.sbuf_tensor("w1s", [128, 2, KC, 128], BF16)),
                        w3s=fctx.enter_context(
                            nc.sbuf_tensor("w3s", [128, 2, KC, 128], BF16)),
                        w2s=fctx.enter_context(
                            nc.sbuf_tensor("w2s", [128, 2, 16, 128], BF16)),
                        wb=fctx.enter_context(
                            nc.sbuf_tensor("wb", [128, T], BF16)),
                    )
                    t_ln = emit_ln()
                    dump_tensor(f"ln1_{d}", x_ln[:], [128, KC, T], BF16)
                    t_wT = emit_routing(t_ln)
                    dump_tensor(f"lg1_{d}", logits[:], [128, TC, E])
                    dump_tensor(f"rt_{d}", rt[:], [128, 6, TC, E])
                    dump_tensor(f"rc_{d}", rc[:], [128, 6, TC, 1])
                    dump_tensor(f"w1r_{d}", wtok[:], [128, TC, E], BF16)
                    emit_ffn(lw, t_wT)
                    plan.fence()
                    dump_tensor(f"ffn1_{d}", x_res[:], [128, KC, T])
                    with nc.Block(f"ffn1_{d}") as blk:
                        plan.flush(blk)
                with ExitStack() as actx:
                    att = dict(
                        A=actx.enter_context(
                            nc.sbuf_tensor("A_T", [128, KC, T], BF16)),
                        qT=actx.enter_context(
                            nc.sbuf_tensor("qT", [128, T], BF16)),
                        qkv_sb=actx.enter_context(
                            nc.sbuf_tensor("qkv_sb", [128, T], BF16)),
                        kT=actx.enter_context(
                            nc.sbuf_tensor("kT", [128, T], BF16)),
                        vT=actx.enter_context(
                            nc.sbuf_tensor("vT", [128, T], BF16)),
                        Qp=actx.enter_context(
                            nc.sbuf_tensor("Qp", [128, 2, T], BF16)),
                        Kp=actx.enter_context(
                            nc.sbuf_tensor("Kp", [128, TC, NF], BF16)),
                        Vx=actx.enter_context(
                            nc.sbuf_tensor("Vx", [128, TC, 65], BF16)),
                        kvx=actx.enter_context(
                            nc.sbuf_tensor("kvx", [128, 2, 65], BF16)),
                        pT=actx.enter_context(
                            nc.sbuf_tensor("pT", [128, 2, 128], BF16)),
                        ph=actx.enter_context(
                            nc.sbuf_tensor("ph", [128, 2, 2, 64], BF16)),
                        wqh=actx.enter_context(
                            nc.sbuf_tensor("wqh", [128, 2, KC, 64], BF16)),
                        wkh=actx.enter_context(
                            nc.sbuf_tensor("wkh", [128, 2, KC, 64], BF16)),
                        wvh=actx.enter_context(
                            nc.sbuf_tensor("wvh", [128, 2, KC, 64], BF16)),
                        woc=actx.enter_context(
                            nc.sbuf_tensor("woc", [128, 2, KC, 128], BF16)),
                    )
                    t_ln = emit_ln()
                    dump_tensor(f"ln2_{d}", x_ln[:], [128, KC, T], BF16)
                    emit_attn(lw, t_ln)
                    plan.fence()
                    dump_tensor(f"A_{d}", att["A"][:], [128, KC, T], BF16)
                    dump_tensor(f"attn_{d}", x_res[:], [128, KC, T])
                    with nc.Block(f"attn_{d}") as blk:
                        plan.flush(blk)
                with ExitStack() as fctx:
                    ffn = dict(
                        h_sb=fctx.enter_context(
                            nc.sbuf_tensor("h_sb2", [128, 16, T], BF16)),
                        w1s=fctx.enter_context(
                            nc.sbuf_tensor("w1s2", [128, 2, KC, 128], BF16)),
                        w3s=fctx.enter_context(
                            nc.sbuf_tensor("w3s2", [128, 2, KC, 128], BF16)),
                        w2s=fctx.enter_context(
                            nc.sbuf_tensor("w2s2", [128, 2, 16, 128], BF16)),
                        wb=fctx.enter_context(
                            nc.sbuf_tensor("wb2", [128, T], BF16)),
                    )
                    t_ln = emit_ln()
                    t_wT = emit_routing(t_ln)
                    emit_ffn(lw, t_wT)
                    plan.fence()
                    dump_tensor(f"ffn2_{d}", x_res[:], [128, KC, T])
                    with nc.Block(f"ffn2_{d}") as blk:
                        plan.flush(blk)

            ev_tok = [None, None, None, None]
            tr_tok = [None, None, None, None]
            t_out = None
            for tc in range(TC):
                for c in range(KC):
                    b = c % 4
                    tr_tok[b] = plan.emit("tensor", TR(
                        A_(0, 128, b * 512, b * 512 + 128),
                        x_res[:, c, tc * 128:(tc + 1) * 128], idf[:]),
                        waits=[ev_tok[b]], inc="T")
                    ev_tok[b] = plan.emit("vector", CP(
                        xio[:, c * 128:(c + 1) * 128],
                        A_(0, 128, b * 512, b * 512 + 128)),
                        waits=[tr_tok[b], t_out if c == 0 else None], inc="V")
                t_out = plan.emit("sync", DMA(
                    out_hbm[tc * 128:(tc + 1) * 128, :], xio),
                    waits=list(ev_tok), inc="IO", amt=16)
            final_io = plan.cnt["IO"]
            plan.emit("sync", lambda e: e.wait_ge(plan.sems["IO"], final_io))
            with nc.Block("out") as blk:
                plan.flush(blk)

        plan.check()
    return dumps


def _run(inputs, dump=(), core_ids=None, trace=False):
    nc = bass.Bass("TRN2", target_bir_lowering=False,
                   detect_race_conditions=False)
    build_program(nc, dump=dump)
    x = np.ascontiguousarray(inputs["L0_attn_x"], dtype=np.float32)
    base = {}
    for d in range(DEPTH):
        p = f"L{d}_"
        base[p + "Wq"] = inputs[p + "attn_Wq"]
        base[p + "Wk"] = inputs[p + "attn_Wk"]
        base[p + "Wv"] = inputs[p + "attn_Wv"]
        base[p + "Wo"] = inputs[p + "attn_Wo"]
        base[p + "bq"] = inputs[p + "attn_bq"]
        base[p + "bk"] = inputs[p + "attn_bk"]
        base[p + "bv"] = inputs[p + "attn_bv"]
        base[p + "bo"] = inputs[p + "attn_bo"]
        base[p + "proj"] = inputs[p + "attn_projection"]
        base[p + "gate"] = inputs[p + "ffn_gate_w"]
        base[p + "w1"] = inputs[p + "ffn_w1"]
        base[p + "w2"] = inputs[p + "ffn_w2"]
        base[p + "w3"] = inputs[p + "ffn_w3"]
    base = {k: np.ascontiguousarray(v, dtype=np.float32)
            for k, v in base.items()}
    if core_ids is None:
        core_ids = list(range(8))
    in_maps = []
    for b in core_ids:
        m = dict(base)
        m["x"] = np.ascontiguousarray(x[b])
        in_maps.append(m)
    res = run_bass_kernel_spmd(nc, in_maps, core_ids=core_ids, trace=trace)
    out = np.stack([np.asarray(r["out"]) for r in res.results],
                   axis=0).astype(np.float32)
    extras = None
    if dump:
        extras = [{k: np.asarray(r["dump_" + k]) for k in dump}
                  for r in res.results]
    return out, extras, res


def kernel(**inputs):
    out, _, _ = _run(inputs)
    return out
